# version 32
# baseline (speedup 1.0000x reference)
import numpy as np
import ml_dtypes
from contextlib import ExitStack

import concourse.bass as bass
import concourse.mybir as mybir
from concourse.bass_utils import run_bass_kernel_spmd

F32 = mybir.dt.float32
BF16 = mybir.dt.bfloat16
FP8 = mybir.dt.float8e4
AF = mybir.ActivationFunctionType
ALU = mybir.AluOpType

D = 1024
S = 4096
H = 16
QW = 512
NT = 4
NCH = [8, 16, 24, 32]
CSH = 20.0
EPS = 1e-5
CONSTS = [-CSH, 0.125, -0.5, 1.0, 0.0, EPS, 1.0 / 1024.0, 2.0, -1.0]
VP = 65 * H


class Prog:
    ENGINES = ("sync", "scalar", "vector", "tensor", "gpsimd")

    def __init__(self, nc):
        self.nc = nc
        self.ops = {e: [] for e in self.ENGINES}
        self.count = {}
        self.sems = {}
        self.last_wait = {e: {} for e in self.ENGINES}

    def add_sem(self, name, handle):
        self.sems[name] = handle
        self.count[name] = 0

    def emit(self, engine, fn, deps=(), inc=None, amt=1):
        flat = []
        for t in deps:
            if t is None:
                continue
            if isinstance(t, list):
                flat.extend(u for u in t if u is not None)
            else:
                flat.append(t)
        tok = None
        if inc is not None:
            self.count[inc] += amt
            tok = (inc, self.count[inc])
        self.ops[engine].append((fn, flat, (inc, amt) if inc else None))
        return tok

    def dma(self, engine, out, in_, sem_name, deps=()):
        def fn(eng, out=out, in_=in_):
            return eng.dma_start(out, in_)
        return self.emit(engine, fn, deps=deps, inc=sem_name, amt=16)

    def replay(self, block):
        used = {}
        for e in self.ENGINES:
            for fn, flat, inc in self.ops[e]:
                for name, val in flat:
                    if val > 0:
                        used.setdefault(name, set()).add(val)
        newval = {}
        for name in self.count:
            pass
        keep = {}
        per_sem_amt = {}
        for e in self.ENGINES:
            prov = {}
            for fn, flat, inc in self.ops[e]:
                if inc is None:
                    continue
                name, amt = inc
                per_sem_amt[name] = amt
        for name, amt in per_sem_amt.items():
            if amt != 1:
                continue
        for name in self.count:
            amt = per_sem_amt.get(name, 1)
            total = self.count[name]
            if amt != 1:
                for v in range(amt, total + 1, amt):
                    keep[(name, v)] = True
                newval[name] = {v: v for v in range(0, total + 1)}
            else:
                u = used.get(name, set())
                m = {0: 0}
                run = 0
                for v in range(1, total + 1):
                    if v in u:
                        run += 1
                        keep[(name, v)] = True
                    else:
                        keep[(name, v)] = False
                    m[v] = run
                newval[name] = m
        for e in self.ENGINES:
            if not self.ops[e]:
                continue
            ops2 = []
            prov_counter = dict.fromkeys(self.count, 0)
            for fn, flat, inc in self.ops[e]:
                inc2 = None
                if inc is not None:
                    name, amt = inc
                    prov_counter[name] += amt
                    v = prov_counter[name]
                    if amt != 1 or keep.get((name, v), False):
                        inc2 = (name, amt)
                waits = []
                for nm, val in flat:
                    nv = newval[nm][min(val, self.count[nm])] if nm in newval else val
                    if nv > 0:
                        waits.append((nm, nv))
                ops2.append((fn, waits, inc2))
            def make(eng_ops):
                def body(eng):
                    lw = {}
                    for fn, waits, inc in eng_ops:
                        for name, val in waits:
                            if lw.get(name, -1) >= val:
                                continue
                            lw[name] = val
                            eng.wait_ge(self.sems[name], val)
                        ins = fn(eng)
                        if inc is not None:
                            name, amt = inc
                            ins.then_inc(self.sems[name], amt)
                return body
            getattr(block, e)(make(ops2))


def build_nc():
    nc = bass.Bass("TRN2", target_bir_lowering=False)
    ap = lambda t: t.ap()

    def _lpr(v, out, in_):
        with nc.allow_low_precision(reason="bf16 softmax reciprocal broadcast"):
            return v.reciprocal(out, in_)

    xqT = ap(nc.dram_tensor("xqT", [D, NT * QW], F32, kind="ExternalInput"))
    xbT = ap(nc.dram_tensor("xbT", [D, S], F32, kind="ExternalInput"))
    wqT_d = ap(nc.dram_tensor("wqT", [D, D], F32, kind="ExternalInput"))
    wkT_d = ap(nc.dram_tensor("wkT", [D, D], F32, kind="ExternalInput"))
    wvT_d = ap(nc.dram_tensor("wvT", [D, D], F32, kind="ExternalInput"))
    woT_d = ap(nc.dram_tensor("woT", [D, D], F32, kind="ExternalInput"))
    fk_d = ap(nc.dram_tensor("fk", [D, D], BF16, kind="ExternalInput"))
    fbias = ap(nc.dram_tensor("fbias", [128, 8], BF16, kind="ExternalInput"))
    scl_d = ap(nc.dram_tensor("scl", [1, 2], F32, kind="ExternalInput"))
    maskh = ap(nc.dram_tensor("maskh", [128, 8 * QW], BF16, kind="ExternalInput"))
    cpool_d = ap(nc.dram_tensor("cpool", [128, len(CONSTS)], F32, kind="ExternalInput"))
    outT = ap(nc.dram_tensor("outT", [D, NT * QW], F32, kind="ExternalOutput"))
    kscr = ap(nc.dram_tensor("kscr", [8, 128, S], BF16, kind="Internal"))
    xb16 = ap(nc.dram_tensor("xb16", [6, 128, 4096], BF16, kind="Internal"))

    p = Prog(nc)
    with ExitStack() as ctx:
        e = ctx.enter_context
        cp_sb = e(nc.sbuf_tensor("cp_sb", [128, len(CONSTS)], F32))
        for i, val in enumerate(CONSTS):
            nc.const_aps.register(F32, val, cp_sb[:, i:i + 1])
        sclbc = e(nc.sbuf_tensor("sclbc", [128, 4], F32))
        bias_sb = e(nc.sbuf_tensor("bias_sb", [128, 8], BF16))
        biasf = e(nc.sbuf_tensor("biasf", [128, 8], F32))
        ones1 = e(nc.sbuf_tensor("ones1", [1, 128], F32))
        onesb = e(nc.sbuf_tensor("onesb", [128, 1], BF16))
        sclrow = e(nc.sbuf_tensor("sclrow", [1, 4], F32))
        mask_sb = e(nc.sbuf_tensor("mask_sb", [128, 8 * QW], BF16))
        arena = e(nc.sbuf_tensor("arena", [128, 16384], BF16))
        fkq = e(nc.sbuf_tensor("fkq", [128, 8 * D], FP8))
        zq = e(nc.sbuf_tensor("zq", [128, 8 * QW], FP8))
        v_sb = e(nc.sbuf_tensor("v_sb", [128, 32 * VP], BF16))
        bfp = e(nc.sbuf_tensor("bfp", [128, 25600], BF16))
        xqs = e(nc.sbuf_tensor("xqs", [128, 1024], F32))
        z_sb = e(nc.sbuf_tensor("z_sb", [128, 8 * QW], F32))
        fpp = e(nc.sbuf_tensor("fpp", [128, 512], F32))
        rfr = e(nc.sbuf_tensor("rfr", [33, 1024], BF16))
        rbc = e(nc.sbuf_tensor("rbc", [64, 2048], BF16))
        rowb = e(nc.sbuf_tensor("rowb", [1, 2048], F32))
        tmpo = e(nc.sbuf_tensor("tmpo", [64, 2048], BF16))

        XQB, XBF1, QT, ATT = 0, 4096, 4096, 8192
        PTE, PTM, KBF, ZB, SQ = 12288, 14336, 16384, 18432, 18944
        WOB, XST = 19456, 23552

        psS = e(nc.psum_tensor("psS", [128, 2048], F32))
        psA = psS[:, 0:1024]
        psB = psS[:, 1024:2048]
        av0 = e(nc.psum_tensor("av0", [128, 512], F32))
        av1 = e(nc.psum_tensor("av1", [128, 512], F32))
        psC = e(nc.psum_tensor("psC", [128, 1024], F32))
        pairs = [psA, psB]

        for s in ("te", "ve", "se", "dK0", "dK1", "dK2", "dK3", "dXQ",
                  "dKS0", "dKS1", "dO", "dM", "dT0", "dT1", "dF0", "dF1",
                  "dXB0", "dXB1", "dW", "dWq", "dWo", "dR", "dXQ1", "dXS0", "dXS1", "dWB",
                  "dR2", "dT2", "dT3", "dXC0", "dXC1"):
            p.add_sem(s, e(nc.semaphore(s)))

        T = lambda fn, deps=(): p.emit("tensor", fn, deps, inc="te")
        V = lambda fn, deps=(): p.emit("vector", fn, deps, inc="ve")
        SC = lambda fn, deps=(): p.emit("scalar", fn, deps, inc="se")

        def xpose_view(dram_ap, j):
            return dram_ap.rearrange("d (n t) -> d n t", t=QW)[:, j, :] \
                          .rearrange("(c p) t -> p c t", p=128)

        def sb3(ap2d):
            return ap2d.rearrange("p (c t) -> p c t", t=QW)

        def wview(dram_ap):
            return dram_ap.rearrange("(c p) o -> p c o", p=128)

        def aview(arena_slice):
            return arena_slice.rearrange("p (c o) -> p c o", o=1024)

        tok_dM = None
        for out_ap, in_ap in ((cp_sb[:], cpool_d), (sclrow[0:1, 0:2], scl_d),
                              (bias_sb[:], fbias), (mask_sb[:], maskh)):
            tok_dM = p.dma("sync", out_ap, in_ap, "dM")
        tok_wk = p.dma("gpsimd", aview(arena[:, 0:8192]), wview(wkT_d), "dW")
        tok_wv = p.dma("gpsimd", aview(arena[:, 8192:16384]), wview(wvT_d), "dW")
        WKV = ("dW", 32)

        t_biasf = V(lambda v: v.tensor_copy(biasf[:], bias_sb[:]), deps=[tok_dM])
        t_ones1 = V(lambda v: v.memset(ones1[:], 1.0))
        t_onesb = V(lambda v: v.memset(onesb[:], 1.0))
        t_vones = t_onesb
        for k in range(32):
            aph = v_sb[:, k * VP:(k + 1) * VP].rearrange("p (h d) -> p h d", h=H)[:, :, 64:65]
            t_vones = V(lambda v, aph=aph: v.memset(aph, 1.0))

        t_r0 = V(lambda v: v.reciprocal(sclrow[0:1, 2:3], sclrow[0:1, 0:1]), deps=[tok_dM])
        t_r1 = V(lambda v: v.reciprocal(sclrow[0:1, 3:4], sclrow[0:1, 1:2]), deps=[t_r0])
        t_r2 = V(lambda v: v.tensor_tensor(sclrow[0:1, 0:1], sclrow[0:1, 0:1],
                                           sclrow[0:1, 1:2], ALU.mult), deps=[t_r1])
        t_bs = T(lambda t: t.matmul(psC[:, 0:4], ones1[:], sclrow[0:1, 0:4],
                                    start=True, stop=True), deps=[t_ones1, t_r2])
        tq0 = V(lambda v: v.tensor_copy(sclbc[:, 0:1], psC[:, 2:3]), deps=[t_bs])
        tq1 = V(lambda v: v.tensor_copy(sclbc[:, 1:2], psC[:, 3:4]), deps=[tq0])
        t_sclbc = V(lambda v: v.tensor_copy(sclbc[:, 2:3], psC[:, 0:1]), deps=[tq1])
        psC_cons = [t_sclbc, t_sclbc]

        xbf_tok = [None, None]
        xbf_cons = [None, None]
        kev_cons = [None, None]
        psA_cons = [None, None]
        psB_cons = [None, None]
        dKS = ["dKS0", "dKS1"]
        dXB = ["dXB0", "dXB1"]
        XBF = [XQB, XBF1]
        t_kv_last = None
        t_k_last = None
        ph1_vready = [None, None]
        first_scalar_dep = tok_dM

        spill_tok = [None, None]
        xspill = [None] * 8
        for tt in range(8):
            s = tt % 2
            xbf_tok[s] = p.dma("gpsimd", sb3(bfp[:, XBF[s]:XBF[s] + 4096]),
                               xpose_view(xbT, tt), dXB[s],
                               deps=[xbf_cons[s], spill_tok[s]])
            if tt >= 2:
                spill_tok[s] = p.dma("sync", xb16[tt - 2],
                                     sb3(bfp[:, XBF[s]:XBF[s] + 4096]),
                                     "dXC0" if s == 0 else "dXC1",
                                     deps=[xbf_tok[s]])
                xspill[tt] = spill_tok[s]
            for oc in range(8):
                hA = oc % 2
                tok = None
                for ic in range(8):
                    st, sp = ic == 0, ic == 7
                    def mmk(t, oc=oc, ic=ic, s=s, hA=hA, st=st, sp=sp):
                        return t.matmul(psS[:, hA * 512:hA * 512 + 512],
                                        arena[:, ic * 1024 + oc * 128: ic * 1024 + oc * 128 + 128],
                                        bfp[:, XBF[s] + ic * 512: XBF[s] + ic * 512 + 512],
                                        start=st, stop=sp)
                    tok = T(mmk, deps=[xbf_tok[s], psA_cons[hA], WKV] if ic == 0 else ())
                t_kv_last = tok
                t_k_last = tok
                ks = oc % 2
                kp = SC(lambda sa, hA=hA, ks=ks:
                        sa.copy(bfp[:, PTE + ks * 1024: PTE + ks * 1024 + 512],
                                psS[:, hA * 512:hA * 512 + 512]),
                        deps=[tok, kev_cons[ks], first_scalar_dep])
                first_scalar_dep = None
                psA_cons[hA] = kp
                kev_cons[ks] = p.dma("sync", kscr[oc, :, tt * 512:(tt + 1) * 512],
                                     bfp[:, PTE + ks * 1024: PTE + ks * 1024 + 512],
                                     dKS[ks], deps=[kp])
                xbf_cons[s] = tok
            for ts4 in (range(4) if tt < 2 else []):
                for oh in range(2):
                    hB = (ts4 * 2 + oh) % 2
                    tok = None
                    for ic in range(8):
                        st, sp = ic == 0, ic == 7
                        def mmv(t, ic=ic, ts4=ts4, oh=oh, s=s, hB=hB, st=st, sp=sp):
                            return t.matmul(psS[:, 1024 + hB * 512:1024 + hB * 512 + 512],
                                            bfp[:, XBF[s] + ic * 512 + ts4 * 128:
                                                XBF[s] + ic * 512 + ts4 * 128 + 128],
                                            arena[:, 8192 + ic * 1024 + oh * 512:
                                                  8192 + ic * 1024 + oh * 512 + 512],
                                            start=st, stop=sp)
                        tok = T(mmv, deps=[xbf_tok[s], psB_cons[hB], WKV] if ic == 0 else ())
                    t_kv_last = tok
                    k = tt * 4 + ts4
                    def vev(v, k=k, oh=oh, hB=hB):
                        oap = v_sb[:, k * VP + oh * 8 * 65: k * VP + (oh * 8 + 8) * 65] \
                            .rearrange("p (h d) -> p h d", h=8)[:, :, 0:64]
                        iap = psS[:, 1024 + hB * 512:1024 + hB * 512 + 512].rearrange("p (h d) -> p h d", h=8)
                        return v.tensor_copy(oap, iap)
                    tok2 = V(vev, deps=[tok])
                    psB_cons[hB] = tok2
                    xbf_cons[s] = tok2
                    ph1_vready[tt] = tok2

        v_done = ("ve", p.count["ve"])

        tok_wq = p.dma("gpsimd", aview(arena[:, 0:8192]), wview(wqT_d), "dWq", deps=[t_k_last])
        WQC = tok_wq
        fkq_tok = None
        fst_cons = [t_kv_last, t_kv_last]
        dF = ["dF0", "dF1"]
        for ic in range(8):
            fs = ic % 2
            tokf = p.dma("sync", bfp[:, ATT + fs * 1024: ATT + fs * 1024 + 1024],
                         fk_d[ic * 128:(ic + 1) * 128, :], dF[fs], deps=[fst_cons[fs]])
            fkq_tok = SC(lambda sa, ic=ic, fs=fs:
                         sa.activation(fkq[:, ic * 1024:(ic + 1) * 1024],
                                       bfp[:, ATT + fs * 1024: ATT + fs * 1024 + 1024],
                                       AF.Copy, scale=sclbc[:, 1:2]),
                         deps=[tokf, t_sclbc, fkq_tok])
            fst_cons[fs] = fkq_tok

        from types import SimpleNamespace
        from collections import deque
        G = SimpleNamespace(
            xq_tok=None, xqb_tok=None, xqb_cons=[t_kv_last, spill_tok[0]],
            q_spill=spill_tok[1],
            qT_cons=None, qT_rdy=None, att_cons=fkq_tok,
            avp_cons=[[None, None], [None, None]],
            pair_cons=[list(filter(None, psA_cons)), list(filter(None, psB_cons))],
            pte_cons=[kev_cons[0], kev_cons[1]], ptm_cons=[None, None],
            kb_tok=[None, None, None, None], kb_cons=[None, None, None, None],
            vready=[ph1_vready[0], ph1_vready[1]] + [None] * 6, xqs_rd=[None, None], xqs_tok=[None] * 8,
            wob_cons=None, xst_tok=[None, None], xst_cons=[None, None],
            tmpo_cons=[None] * 4, rowb_cons=None, bfr_cons=None,
            rbc_cons=[None, None], srow_cons=None, lm_toks=[], pend_apply=[],
            vstage_tok={},
            zb_cons=fkq_tok, sq_cons=None, fpp_cons=[None, None],
            psC_cons=psC_cons, out_tok=None, res_toks=None, z2_toks=None,
            ln1_toks=None, q_toks=None, zq_gate=None, xq_gate=None,
        )
        dKb = ["dK0", "dK1", "dK2", "dK3"]
        dTm = ["dT0", "dT1", "dT2", "dT3"]
        dKS_full = [("dKS0", 16 * 32), ("dKS1", 16 * 32)]
        fill = deque()

        def emit_xqb(j):
            G.xqb_tok = p.dma("gpsimd", sb3(bfp[:, XQB:XQB + 4096]),
                              xpose_view(xqT, j), "dXB0", deps=[G.xqb_cons])

        def emit_q(j):
            qev = G.q_spill
            G.q_spill = None
            for oc in range(8):
                hC = oc % 2
                tok = None
                for ic in range(8):
                    st, sp = ic == 0, ic == 7
                    def mmq(t, oc=oc, ic=ic, hC=hC, st=st, sp=sp):
                        return t.matmul(psC[:, hC * 512:hC * 512 + 512],
                                        arena[:, ic * 1024 + oc * 128: ic * 1024 + oc * 128 + 128],
                                        bfp[:, XQB + ic * 512: XQB + ic * 512 + 512],
                                        start=st, stop=sp)
                    tok = T(mmq, deps=[G.xqb_tok, WQC, G.psC_cons[hC], G.qT_cons]
                            if ic == 0 else ())
                G.xqb_cons = tok
                qev = SC(lambda sa, oc=oc, hC=hC:
                         sa.copy(bfp[:, QT + oc * 512: QT + oc * 512 + 512],
                                 psC[:, hC * 512:hC * 512 + 512]),
                         deps=[tok, qev])
                G.psC_cons[hC] = qev
            G.qT_rdy = qev

        def emit_attention(j):
            n = NCH[j]
            G.lm_toks = []
            src = wview(woT_d)[:, :, 0:512]
            dst = bfp[:, WOB:WOB + 4096].rearrange("p (c o) -> p c o", o=512)
            G.wo_pre = p.dma("gpsimd", dst, src, "dWB",
                             deps=[G.wob_cons, t_kv_last])
            for gp in range(4):
                streams = [
                    (2 * gp, psA, PTE, PTM, (0, 1), 0),
                    (2 * gp + 1, psB, PTE + 1024, PTM + 1024, (2, 3), 1),
                ]
                pend = [None, None]
                avs = [[None, None], [None, None]]

                def flush_one(si):
                    if pend[si] is None:
                        return
                    c, g_, src_, st, sp, av_dep, par_ = pend[si]
                    pend[si] = None
                    if par_ == 0:
                        A0, A1 = av0[0:65, :], av1[0:65, :]
                    else:
                        A0, A1 = psC[0:65, 0:512], psC[0:65, 512:1024]
                    gate0 = ([G.avp_cons[par_][0]] +
                             ([G.psC_cons[0]] if par_ == 1 else [])) if st else []
                    gate1 = ([G.avp_cons[par_][1]] +
                             ([G.psC_cons[1]] if par_ == 1 else [])) if st else []
                    gate0 = gate0 + [G.vready[c // 4]]
                    gate1 = gate1 + ([G.vready[c // 4]] if st else [])
                    t_a0 = T(lambda t, c=c, g=g_, src=src_, st=st, sp=sp, A0=A0:
                             t.matmul(A0,
                                      v_sb[:, c * VP + (2 * g) * 65:
                                           c * VP + (2 * g) * 65 + 65],
                                      bfp[:, src:src + 512], start=st, stop=sp),
                             deps=[av_dep] + gate0)
                    t_a1 = T(lambda t, c=c, g=g_, src=src_, st=st, sp=sp, A1=A1:
                             t.matmul(A1,
                                      v_sb[:, c * VP + (2 * g + 1) * 65:
                                           c * VP + (2 * g + 1) * 65 + 65],
                                      bfp[:, src + 512:src + 1024], start=st, stop=sp),
                             deps=gate1)
                    if src_ >= PTM and src_ < KBF:
                        G.ptm_cons[si] = t_a1
                    else:
                        G.pte_cons[si] = t_a1
                    if sp:
                        avs[si] = [t_a0, t_a1]

                for b in range(n // 4):
                    kb_toks = []
                    for si, (g_, P, pte_o, ptm_o, kbs, par_) in enumerate(streams):
                        ks = kbs[b % 2]
                        G.kb_tok[ks] = p.dma(
                            "sync", bfp[:, KBF + ks * 512: KBF + ks * 512 + 512],
                            kscr[g_, :, b * 512:(b + 1) * 512], dKb[ks],
                            deps=[G.kb_cons[ks]] + dKS_full)
                    for c4 in range(4):
                        c = 4 * b + c4
                        so_toks = [None, None]
                        for si, (g_, P, pte_o, ptm_o, kbs, par_) in enumerate(streams):
                            ks = kbs[b % 2]
                            t_se = T(lambda t, ks=ks, c4=c4, g=g_, P=P:
                                     t.matmul(P[:, 0:512],
                                              bfp[0:64, KBF + ks * 512 + c4 * 128:
                                                  KBF + ks * 512 + c4 * 128 + 128],
                                              bfp[0:64, QT + g * 512: QT + g * 512 + 512],
                                              start=True, stop=True),
                                     deps=[G.kb_tok[ks], G.qT_rdy, G.pair_cons[si]])
                            t_so = T(lambda t, ks=ks, c4=c4, g=g_, P=P:
                                     t.matmul(P[:, 512:1024],
                                              bfp[64:128, KBF + ks * 512 + c4 * 128:
                                                  KBF + ks * 512 + c4 * 128 + 128],
                                              bfp[64:128, QT + g * 512: QT + g * 512 + 512],
                                              start=True, stop=True, tile_position=(64, 0)))
                            so_toks[si] = t_so
                            G.kb_cons[ks] = t_so
                        flush_one(0)
                        flush_one(1)
                        for si, (g_, P, pte_o, ptm_o, kbs, par_) in enumerate(streams):
                            t_exp = SC(lambda sa, P=P, pte_o=pte_o:
                                       sa.activation(bfp[:, pte_o:pte_o + 1024],
                                                     P[:, 0:1024], AF.Exp,
                                                     bias=-CSH, scale=0.125),
                                       deps=[so_toks[si], G.pte_cons[si]])
                            G.pair_cons[si] = t_exp
                            src_ = pte_o
                            if c >= n - 8:
                                m = c - (n - 8)
                                t_m0 = V(lambda v, src=src_, ptm_o=ptm_o, m=m:
                                         v.tensor_tensor(bfp[:, ptm_o:ptm_o + 512],
                                                         bfp[:, src:src + 512],
                                                         mask_sb[:, m * 512:(m + 1) * 512],
                                                         ALU.mult),
                                         deps=[t_exp, G.ptm_cons[si], tok_dM])
                                t_m1 = V(lambda v, src=src_, ptm_o=ptm_o, m=m:
                                         v.tensor_tensor(bfp[:, ptm_o + 512:ptm_o + 1024],
                                                         bfp[:, src + 512:src + 1024],
                                                         mask_sb[:, m * 512:(m + 1) * 512],
                                                         ALU.mult),
                                         deps=[t_m0])
                                G.pte_cons[si] = t_m1
                                av_dep = t_m1
                                src_ = ptm_o
                            else:
                                av_dep = t_exp
                            st, sp = c == 0, c == n - 1
                            pend[si] = (c, g_, src_, st, sp, av_dep, par_)
                flush_one(0)
                flush_one(1)
                ecs = [None, None]
                sss = [None, None]
                for si, (g_, P, pte_o, ptm_o, kbs, par_) in enumerate(streams):
                    if par_ == 0:
                        R0, R1 = av0[64:65, :], av1[64:65, :]
                        D0, D1 = av0[0:64, :], av1[0:64, :]
                    else:
                        R0, R1 = psC[64:65, 0:512], psC[64:65, 512:1024]
                        D0, D1 = psC[0:64, 0:512], psC[0:64, 512:1024]
                    so = si * 1024
                    t_s0 = V(lambda v, R0=R0, so=so:
                             v.tensor_copy(rowb[0:1, so:so + 512], R0),
                             deps=[avs[si][0], G.rowb_cons])
                    t_s1 = V(lambda v, R1=R1, so=so:
                             v.tensor_copy(rowb[0:1, so + 512:so + 1024], R1),
                             deps=[avs[si][1]])
                    sss[si] = (t_s0, t_s1)
                    t_ec0 = V(lambda v, g=g_, D0=D0: v.tensor_copy(
                        bfp[0:64, ATT + g * 512: ATT + g * 512 + 512], D0),
                        deps=[G.att_cons])
                    ts_ = g_ % 4
                    t_ec1 = V(lambda v, ts_=ts_, D1=D1: v.tensor_copy(
                        tmpo[0:64, ts_ * 512: ts_ * 512 + 512], D1),
                        deps=[G.tmpo_cons[ts_]])
                    ecs[si] = (t_ec0, t_ec1)
                    G.avp_cons[par_] = [t_ec0, t_ec1]
                    if par_ == 1:
                        G.psC_cons[0] = t_ec0
                        G.psC_cons[1] = t_ec1

                for ap_ in G.pend_apply:
                    ap_()
                G.pend_apply = []

                def mk_apply(si, g_, ts_, t_dR, ec0, ec1):
                    def go():
                        t_lo = V(lambda v: v.tensor_tensor(
                            bfp[0:64, ATT + g_ * 512: ATT + g_ * 512 + 512],
                            bfp[0:64, ATT + g_ * 512: ATT + g_ * 512 + 512],
                            rbc[0:64, si * 1024: si * 1024 + 512], ALU.mult),
                            deps=[t_dR, ec0])
                        t_up = V(lambda v: v.tensor_tensor(
                            tmpo[0:64, ts_ * 512: ts_ * 512 + 512],
                            tmpo[0:64, ts_ * 512: ts_ * 512 + 512],
                            rbc[0:64, si * 1024 + 512: si * 1024 + 1024],
                            ALU.mult), deps=[t_lo, ec1])
                        G.tmpo_cons[ts_] = p.dma(
                            "sync", bfp[64:128, ATT + g_ * 512: ATT + g_ * 512 + 512],
                            tmpo[0:64, ts_ * 512: ts_ * 512 + 512], dTm[ts_],
                            deps=[t_up, G.att_cons])
                        G.lm_toks.append(t_lo)
                        G.rbc_cons[si] = t_up
                    return go

                if gp < 3:
                    for si, (g_, P, pte_o, ptm_o, kbs, par_) in enumerate(streams):
                        row = 0 if si == 0 else 32
                        t_ra = V(lambda v, si=si, row=row: _lpr(
                            v, rfr[row:row + 1, 0:512],
                            rowb[0:1, si * 1024: si * 1024 + 512]),
                            deps=[sss[si][0]] + ([G.bfr_cons] if si == 0 else []))
                        t_rb = V(lambda v, si=si, row=row: _lpr(
                            v, rfr[row:row + 1, 512:1024],
                            rowb[0:1, si * 1024 + 512: si * 1024 + 1024]),
                            deps=[sss[si][1]])
                        t_dR = p.dma(
                            "gpsimd", rbc[0:64, si * 1024:(si + 1) * 1024],
                            bass.AP(rfr, row * 1024,
                                    [[1024, 1], [0, 64], [1, 1024]]),
                            "dR" if si == 0 else "dR2",
                            deps=[t_rb, G.rbc_cons[si]])
                        G.bfr_cons = t_dR
                        G.pend_apply.append(
                            mk_apply(si, g_, g_ % 4, t_dR, ecs[si][0], ecs[si][1]))
                else:
                    t_ln0 = SC(lambda sa: sa.activation(rowb[0:1, 0:1024],
                                                        rowb[0:1, 0:1024], AF.Ln),
                               deps=[sss[1][1], G.rowb_cons])
                    t_ex0 = SC(lambda sa: sa.activation(rowb[0:1, 0:1024],
                                                        rowb[0:1, 0:1024], AF.Exp,
                                                        scale=-1.0), deps=[t_ln0])
                    t_ln1 = SC(lambda sa: sa.activation(rowb[0:1, 1024:2048],
                                                        rowb[0:1, 1024:2048], AF.Ln),
                               deps=[t_ex0])
                    t_ex1 = SC(lambda sa: sa.activation(rowb[0:1, 1024:2048],
                                                        rowb[0:1, 1024:2048], AF.Exp,
                                                        scale=-1.0), deps=[t_ln1])
                    t_bA0 = T(lambda t: t.matmul(psC[0:64, 0:512], ones1[0:1, 0:64],
                                                 rowb[0:1, 0:512],
                                                 start=True, stop=True),
                              deps=[t_ex0, ecs[1][0]])
                    t_bB0 = T(lambda t: t.matmul(psC[0:64, 512:1024], ones1[0:1, 0:64],
                                                 rowb[0:1, 512:1024],
                                                 start=True, stop=True),
                              deps=[ecs[1][1]])
                    t_bA1 = T(lambda t: t.matmul(av0[0:64, 0:512], ones1[0:1, 0:64],
                                                 rowb[0:1, 1024:1536],
                                                 start=True, stop=True),
                              deps=[t_ex1, ecs[0][0]])
                    t_bB1 = T(lambda t: t.matmul(av1[0:64, 0:512], ones1[0:1, 0:64],
                                                 rowb[0:1, 1536:2048],
                                                 start=True, stop=True),
                              deps=[ecs[0][1]])
                    G.rowb_cons = t_bB1
                    for si, (g_, P, pte_o, ptm_o, kbs, par_) in enumerate(streams):
                        ts_ = g_ % 4
                        PL = psC[0:64, 0:512] if si == 0 else av0[0:64, 0:512]
                        PU = psC[0:64, 512:1024] if si == 0 else av1[0:64, 0:512]
                        bl = t_bA0 if si == 0 else t_bA1
                        bu = t_bB0 if si == 0 else t_bB1
                        t_lo = V(lambda v, g=g_, PL=PL: v.tensor_tensor(
                            bfp[0:64, ATT + g * 512: ATT + g * 512 + 512],
                            bfp[0:64, ATT + g * 512: ATT + g * 512 + 512],
                            PL, ALU.mult), deps=[bl, ecs[si][0]])
                        t_up = V(lambda v, ts_=ts_, PU=PU: v.tensor_tensor(
                            tmpo[0:64, ts_ * 512: ts_ * 512 + 512],
                            tmpo[0:64, ts_ * 512: ts_ * 512 + 512],
                            PU, ALU.mult), deps=[bu, ecs[si][1]])
                        G.tmpo_cons[ts_] = p.dma(
                            "sync", bfp[64:128, ATT + g_ * 512: ATT + g_ * 512 + 512],
                            tmpo[0:64, ts_ * 512: ts_ * 512 + 512], dTm[ts_],
                            deps=[t_up, G.att_cons])
                        G.lm_toks.append(t_lo)
                        if si == 0:
                            G.psC_cons[0] = t_lo
                            G.psC_cons[1] = t_up
                        else:
                            G.avp_cons[0] = [t_lo, t_up]
                budget = n // 8 + 2
                for _ in range(budget):
                    if fill:
                        fill.popleft()()
            G.qT_cons = ("te", p.count["te"])

        def emit_O(j):
            att_rdy = list(G.lm_toks) + [(dn, p.count[dn]) for dn in dTm]
            res_toks = []
            wo_tok = None
            for og in range(8):
                if og == 0:
                    wo_tok = G.wo_pre
                elif og == 4:
                    src = wview(woT_d)[:, :, 512:1024]
                    dst = bfp[:, WOB:WOB + 4096].rearrange("p (c o) -> p c o", o=512)
                    wo_tok = p.dma("gpsimd", dst, src, "dWB",
                                   deps=[G.wob_cons, t_kv_last])
                hC = og % 2
                AV = av0 if hC == 0 else av1
                tok = None
                for ic in range(8):
                    st, sp = ic == 0, ic == 7
                    def mmo(t, og=og, ic=ic, AV=AV, st=st, sp=sp):
                        return t.matmul(AV[:, 0:512],
                                        bfp[:, WOB + ic * 512 + (og % 4) * 128:
                                            WOB + ic * 512 + (og % 4) * 128 + 128],
                                        bfp[:, ATT + ic * 512: ATT + ic * 512 + 512],
                                        start=st, stop=sp)
                    tok = T(mmo, deps=(att_rdy + [wo_tok, G.avp_cons[0][hC]])
                            if ic == 0 else ())
                G.att_cons = tok
                if og % 4 == 3:
                    G.wob_cons = tok
                xp = og % 2
                xq_t = p.dma("sync", xqs[:, xp * 512:(xp + 1) * 512],
                             xqT[og * 128:(og + 1) * 128, j * QW:(j + 1) * QW],
                             "dXQ" if xp == 0 else "dXQ1",
                             deps=[G.xqs_rd[xp], G.out_tok])
                t_res = V(lambda v, og=og, AV=AV, xp=xp:
                          v.tensor_tensor(z_sb[:, og * 512:(og + 1) * 512],
                                          AV[:, 0:512],
                                          xqs[:, xp * 512:(xp + 1) * 512], ALU.add),
                          deps=[tok, xq_t, G.out_tok])
                G.xqs_rd[xp] = t_res
                G.avp_cons[0][hC] = t_res
                res_toks.append(t_res)
            G.res_toks = res_toks

        def step_ln_pieces(z_src_attr, quantize):
            gen_box = []

            def go_all():
                g = _ln_gen(z_src_attr, quantize)
                for _ in g:
                    pass
            return [go_all]

        def _ln_gen(z_src_attr, quantize):
            z_src = getattr(G, z_src_attr)
            s1tok = G.zb_cons
            for cidx in range(8):
                zc = SC(lambda sa, cidx=cidx:
                        sa.activation(bfp[:, ZB:ZB + 512],
                                      z_sb[:, cidx * 512:(cidx + 1) * 512], AF.Copy),
                        deps=[z_src[cidx], s1tok])
                s1tok = T(lambda t, cidx=cidx:
                          t.matmul(av0[0:1, 0:512], onesb[:], bfp[:, ZB:ZB + 512],
                                   start=(cidx == 0), stop=(cidx == 7)),
                          deps=[zc, t_onesb] + ([G.avp_cons[0][0]] if cidx == 0 else []))
            G.zb_cons = s1tok
            s2tok = G.sq_cons
            for cidx in range(8):
                sc_ = V(lambda v, cidx=cidx:
                        v.tensor_tensor(bfp[:, SQ:SQ + 512],
                                        z_sb[:, cidx * 512:(cidx + 1) * 512],
                                        z_sb[:, cidx * 512:(cidx + 1) * 512], ALU.mult),
                        deps=[z_src[cidx], s2tok])
                s2tok = T(lambda t, cidx=cidx:
                          t.matmul(av1[0:1, 0:512], onesb[:], bfp[:, SQ:SQ + 512],
                                   start=(cidx == 0), stop=(cidx == 7)),
                          deps=[sc_] + ([G.avp_cons[0][1]] if cidx == 0 else []))
            G.sq_cons = s2tok
            t_m = V(lambda v: v.tensor_scalar(rowb[0:1, 0:512], av0[0:1, 0:512],
                                              1.0 / 1024.0, None, ALU.mult),
                    deps=[s1tok, G.rowb_cons])
            t_vr = V(lambda v: v.tensor_scalar(rowb[0:1, 512:1024], av1[0:1, 0:512],
                                               1.0 / 1024.0, EPS, ALU.mult, ALU.add),
                     deps=[s2tok, t_m])
            G.avp_cons[0][0] = t_m
            G.avp_cons[0][1] = t_vr
            yield
            t_msq = V(lambda v: v.tensor_tensor(rowb[0:1, 1024:1536], rowb[0:1, 0:512],
                                                rowb[0:1, 0:512], ALU.mult),
                      deps=[t_vr])
            t_v2 = V(lambda v: v.tensor_tensor(rowb[0:1, 1536:2048], rowb[0:1, 512:1024],
                                               rowb[0:1, 1024:1536], ALU.subtract),
                     deps=[t_msq])
            t_ln = SC(lambda sa: sa.activation(rowb[0:1, 1024:1536], rowb[0:1, 1536:2048],
                                               AF.Ln), deps=[t_v2])
            t_rstd = SC(lambda sa: sa.activation(rowb[0:1, 512:1024], rowb[0:1, 1024:1536],
                                                 AF.Exp, scale=-0.5),
                        deps=[t_ln])
            t_mr = V(lambda v: v.tensor_tensor(rowb[0:1, 1536:2048], rowb[0:1, 0:512],
                                               rowb[0:1, 512:1024], ALU.mult),
                     deps=[t_rstd])
            t_bb0 = T(lambda t: t.matmul(av0[:, 0:512], ones1[:], rowb[0:1, 512:1024],
                                         start=True, stop=True),
                      deps=[t_rstd, t_m, G.avp_cons[0][0]])
            t_bb1 = T(lambda t: t.matmul(av1[:, 0:512], ones1[:], rowb[0:1, 1536:2048],
                                         start=True, stop=True),
                      deps=[t_mr, t_vr, G.avp_cons[0][1]])
            G.rowb_cons = t_bb1
            new_toks = []
            t_zq = None
            for cidx in range(8):
                fs = 0
                t_pa = V(lambda v, cidx=cidx, fs=fs:
                         v.tensor_tensor(fpp[:, fs * 512: fs * 512 + 512],
                                         z_sb[:, cidx * 512:(cidx + 1) * 512],
                                         av0[:, 0:512], ALU.mult),
                         deps=[t_bb0, G.fpp_cons[fs]] +
                              ([s1tok, s2tok] if cidx == 0 else []))
                t_pb = V(lambda v, cidx=cidx, fs=fs:
                         v.tensor_tensor(z_sb[:, cidx * 512:(cidx + 1) * 512],
                                         fpp[:, fs * 512: fs * 512 + 512],
                                         av1[:, 0:512], ALU.subtract),
                         deps=[t_pa, t_bb1])
                G.fpp_cons[fs] = t_pb
                new_toks.append(t_pb)
            G.avp_cons[0][0] = new_toks[-1]
            G.avp_cons[0][1] = new_toks[-1]
            if quantize:
                G.ln1_toks = new_toks
            else:
                G.out_chunks = new_toks
            yield
            if quantize:
                for cidx in range(8):
                    t_zb = V(lambda v, cidx=cidx:
                             v.tensor_copy(bfp[:, ZB:ZB + 512],
                                           z_sb[:, cidx * 512:(cidx + 1) * 512]),
                             deps=[new_toks[cidx], G.zb_cons])
                    t_zq = SC(lambda sa, cidx=cidx:
                              sa.activation(zq[:, cidx * 512:(cidx + 1) * 512],
                                            bfp[:, ZB:ZB + 512], AF.Copy,
                                            scale=sclbc[:, 0:1]),
                              deps=[t_zb, t_sclbc, G.zq_gate])
                    G.zb_cons = t_zq
                G.q_last = t_zq
            yield

        def step_vstage(T_, s4):
            def go():
                xs = s4 % 2
                src = xb16[T_ - 2].rearrange("p (c t) -> p c t", t=QW)[
                    :, :, s4 * 128:(s4 + 1) * 128]
                dst = bfp[:, XST + xs * 1024: XST + xs * 1024 + 1024] \
                    .rearrange("p (c t) -> p c t", t=128)
                G.vstage_tok[(T_, s4)] = p.dma(
                    "sync", dst, src, "dXS0" if xs == 0 else "dXS1",
                    deps=[G.xst_cons[xs], xspill[T_]])
            return go

        def step_vchain(T_, s4):
            def go():
                xs = s4 % 2
                tok_x = G.vstage_tok.pop((T_, s4))
                evs = []
                for oh in range(2):
                    AV = av0 if oh == 0 else av1
                    tok = None
                    for ic in range(8):
                        st, sp = ic == 0, ic == 7
                        def mmv2(t, ic=ic, oh=oh, xs=xs, AV=AV, st=st, sp=sp):
                            return t.matmul(AV[:, 0:512],
                                            bfp[:, XST + xs * 1024 + ic * 128:
                                                XST + xs * 1024 + ic * 128 + 128],
                                            arena[:, 8192 + ic * 1024 + oh * 512:
                                                  8192 + ic * 1024 + oh * 512 + 512],
                                            start=st, stop=sp)
                        tok = T(mmv2, deps=[tok_x, G.avp_cons[0][oh]] if ic == 0 else ())
                    k = T_ * 4 + s4

                    def vev2(v, k=k, oh=oh, AV=AV):
                        oap = v_sb[:, k * VP + oh * 8 * 65: k * VP + (oh * 8 + 8) * 65] \
                            .rearrange("p (h d) -> p h d", h=8)[:, :, 0:64]
                        iap = AV[:, 0:512].rearrange("p (h d) -> p h d", h=8)
                        return v.tensor_copy(oap, iap)
                    ev = V(vev2, deps=[tok])
                    G.avp_cons[0][oh] = ev
                    evs.append(ev)
                G.xst_cons[xs] = evs[-1]
                if s4 == 3:
                    G.vready[T_] = evs[-1]
            return go

        def append_weave(tiles):
            steps = [(T_, s4) for T_ in tiles for s4 in range(4)]
            si_, ci = 0, 0
            while ci < len(steps):
                if si_ < len(steps) and si_ - ci < 2:
                    fill.append(step_vstage(*steps[si_])); si_ += 1
                else:
                    fill.append(step_vchain(*steps[ci])); ci += 1

        def step_ffn(og):
            def go():
                hC = og % 2
                AV = av0 if hC == 0 else av1
                tok = None
                for ic in range(8):
                    st, sp = ic == 0, ic == 7
                    def mmf(t, og=og, ic=ic, AV=AV, st=st, sp=sp):
                        return t.matmul(AV[:, 0:512],
                                        fkq[:, ic * 1024 + og * 128:
                                            ic * 1024 + og * 128 + 128],
                                        zq[:, ic * 512:(ic + 1) * 512],
                                        start=st, stop=sp)
                    tok = T(mmf, deps=[G.q_last, fkq_tok, G.avp_cons[0][hC]] if ic == 0 else ())
                fs = 0
                t_yb = V(lambda v, og=og, AV=AV:
                         v.tensor_scalar(bfp[:, SQ:SQ + 512],
                                         AV[:, 0:512],
                                         sclbc[:, 2:3], biasf[:, og:og + 1],
                                         ALU.mult, ALU.add),
                         deps=[tok, G.sq_cons, t_biasf])
                t_z2 = V(lambda v, og=og, fs=fs:
                         v.tensor_tensor(fpp[:, fs * 512: fs * 512 + 512],
                                         z_sb[:, og * 512:(og + 1) * 512],
                                         bfp[:, SQ:SQ + 512], ALU.add),
                         deps=[t_yb, G.ln1_toks[og], G.fpp_cons[fs]])
                t_cp = V(lambda v, og=og, fs=fs:
                         v.tensor_copy(z_sb[:, og * 512:(og + 1) * 512],
                                       fpp[:, fs * 512: fs * 512 + 512]),
                         deps=[t_z2])
                G.fpp_cons[fs] = t_cp
                G.sq_cons = t_z2
                G.avp_cons[0][hC] = t_z2
                if og == 0:
                    G.z2_toks = []
                G.z2_toks.append(t_cp)
                if og == 7:
                    G.zq_gate = tok
            return go

        def step_out(j):
            def go():
                tok_o = p.dma("sync", xpose_view(outT, j), sb3(z_sb[:]), "dO",
                              deps=G.out_chunks)
                G.out_tok = tok_o
            return go

        emit_xqb(0)
        emit_q(0)
        emit_xqb(1)
        append_weave((2, 3))
        for j in range(NT):
            emit_attention(j)
            while fill:
                fill.popleft()()
            if j + 1 < NT:
                emit_q(j + 1)
            emit_O(j)
            if j + 2 < NT:
                emit_xqb(j + 2)
            for piece in step_ln_pieces("res_toks", True):
                fill.append(piece)
            for og in range(8):
                fill.append(step_ffn(og))
            for piece in step_ln_pieces("z2_toks", False):
                fill.append(piece)
            fill.append(step_out(j))
            if j + 1 < NT and 2 * j + 4 < 8:
                append_weave((2 * j + 4, 2 * j + 5))
        while fill:
            fill.popleft()()

        p.emit("sync", lambda s: s.nop(), deps=[("dO", 16 * NT)])

        block = e(nc.Block())
        p.replay(block)
    return nc


_NC_CACHE = {}


def _get_nc():
    if "nc" not in _NC_CACHE:
        _NC_CACHE["nc"] = build_nc()
    return _NC_CACHE["nc"]


def _make_masks(half):
    r = np.arange(128)[:, None]
    c = np.arange(512)[None, :]
    tiles = []
    for m in range(8):
        if half == 0:
            t = (c - r >= 128 * m) if m < 4 else np.zeros((128, 512), bool)
        else:
            t = np.ones((128, 512), bool) if m < 4 else (c - r >= 128 * (m - 4))
        tiles.append(t.astype(ml_dtypes.bfloat16))
    return np.concatenate(tiles, axis=1)


def kernel(L0_attn_in_features, L0_attn_Wq, L0_attn_Wk, L0_attn_Wv, L0_attn_Wo,
           L0_ffn_kernel, L0_ffn_input_scale, L0_ffn_kernel_scale, L0_ffn_bias):
    x = np.asarray(L0_attn_in_features)
    wq = np.ascontiguousarray(np.asarray(L0_attn_Wq).T)
    wk = np.ascontiguousarray(np.asarray(L0_attn_Wk).T)
    wv = np.ascontiguousarray(np.asarray(L0_attn_Wv).T)
    wo = np.ascontiguousarray(np.asarray(L0_attn_Wo).T)
    fk = np.ascontiguousarray(np.asarray(L0_ffn_kernel))
    fbias = np.ascontiguousarray(np.asarray(L0_ffn_bias).reshape(8, 128).T)
    scl = np.concatenate([np.asarray(L0_ffn_input_scale),
                          np.asarray(L0_ffn_kernel_scale)]).reshape(1, 2).astype(np.float32)
    cpool = np.zeros((128, len(CONSTS)), np.float32)
    for i, val in enumerate(CONSTS):
        cpool[:, i] = val
    masks = [_make_masks(0), _make_masks(1)]

    in_maps = []
    for c in range(8):
        b, h = c // 2, c % 2
        xbT = np.ascontiguousarray(x[b].T)
        tiles = [2 * jj + h for jj in range(4)]
        xqT = np.ascontiguousarray(xbT.reshape(D, 8, QW)[:, tiles, :].reshape(D, NT * QW))
        in_maps.append({
            "xqT": xqT, "xbT": xbT, "wqT": wq, "wkT": wk, "wvT": wv, "woT": wo,
            "fk": fk, "fbias": fbias, "scl": scl, "maskh": masks[h], "cpool": cpool,
        })

    nc = _get_nc()
    res = run_bass_kernel_spmd(nc, in_maps, core_ids=list(range(8)))

    out = np.empty((4, S, D), np.float32)
    for c in range(8):
        b, h = c // 2, c % 2
        oT = np.asarray(res.results[c]["outT"])
        o = oT.T.reshape(NT, QW, D)
        for jj in range(NT):
            t = 2 * jj + h
            out[b, t * QW:(t + 1) * QW, :] = o[jj]
    return out


# version 33
# speedup vs baseline: 1.0277x; 1.0277x over previous
import numpy as np
import ml_dtypes
from contextlib import ExitStack

import concourse.bass as bass
import concourse.mybir as mybir
from concourse.bass_utils import run_bass_kernel_spmd

F32 = mybir.dt.float32
BF16 = mybir.dt.bfloat16
FP8 = mybir.dt.float8e4
AF = mybir.ActivationFunctionType
ALU = mybir.AluOpType

D = 1024
S = 4096
H = 16
QW = 512
NT = 4
NCH = [8, 16, 24, 32]
CSH = 20.0
EPS = 1e-5
CONSTS = [-CSH, 0.125, -0.5, 1.0, 0.0, EPS, 1.0 / 1024.0, 2.0, -1.0]
VP = 65 * H


class Prog:
    ENGINES = ("sync", "scalar", "vector", "tensor", "gpsimd")

    def __init__(self, nc):
        self.nc = nc
        self.ops = {e: [] for e in self.ENGINES}
        self.count = {}
        self.sems = {}
        self.last_wait = {e: {} for e in self.ENGINES}

    def add_sem(self, name, handle):
        self.sems[name] = handle
        self.count[name] = 0

    def emit(self, engine, fn, deps=(), inc=None, amt=1):
        flat = []
        for t in deps:
            if t is None:
                continue
            if isinstance(t, list):
                flat.extend(u for u in t if u is not None)
            else:
                flat.append(t)
        tok = None
        if inc is not None:
            self.count[inc] += amt
            tok = (inc, self.count[inc])
        self.ops[engine].append((fn, flat, (inc, amt) if inc else None))
        return tok

    def dma(self, engine, out, in_, sem_name, deps=()):
        def fn(eng, out=out, in_=in_):
            return eng.dma_start(out, in_)
        return self.emit(engine, fn, deps=deps, inc=sem_name, amt=16)

    def replay(self, block):
        used = {}
        for e in self.ENGINES:
            for fn, flat, inc in self.ops[e]:
                for name, val in flat:
                    if val > 0:
                        used.setdefault(name, set()).add(val)
        newval = {}
        for name in self.count:
            pass
        keep = {}
        per_sem_amt = {}
        for e in self.ENGINES:
            prov = {}
            for fn, flat, inc in self.ops[e]:
                if inc is None:
                    continue
                name, amt = inc
                per_sem_amt[name] = amt
        for name, amt in per_sem_amt.items():
            if amt != 1:
                continue
        for name in self.count:
            amt = per_sem_amt.get(name, 1)
            total = self.count[name]
            if amt != 1:
                for v in range(amt, total + 1, amt):
                    keep[(name, v)] = True
                newval[name] = {v: v for v in range(0, total + 1)}
            else:
                u = used.get(name, set())
                m = {0: 0}
                run = 0
                for v in range(1, total + 1):
                    if v in u:
                        run += 1
                        keep[(name, v)] = True
                    else:
                        keep[(name, v)] = False
                    m[v] = run
                newval[name] = m
        for e in self.ENGINES:
            if not self.ops[e]:
                continue
            ops2 = []
            prov_counter = dict.fromkeys(self.count, 0)
            for fn, flat, inc in self.ops[e]:
                inc2 = None
                if inc is not None:
                    name, amt = inc
                    prov_counter[name] += amt
                    v = prov_counter[name]
                    if amt != 1 or keep.get((name, v), False):
                        inc2 = (name, amt)
                waits = []
                for nm, val in flat:
                    nv = newval[nm][min(val, self.count[nm])] if nm in newval else val
                    if nv > 0:
                        waits.append((nm, nv))
                ops2.append((fn, waits, inc2))
            def make(eng_ops):
                def body(eng):
                    lw = {}
                    for fn, waits, inc in eng_ops:
                        for name, val in waits:
                            if lw.get(name, -1) >= val:
                                continue
                            lw[name] = val
                            eng.wait_ge(self.sems[name], val)
                        ins = fn(eng)
                        if inc is not None:
                            name, amt = inc
                            ins.then_inc(self.sems[name], amt)
                return body
            getattr(block, e)(make(ops2))


def build_nc():
    nc = bass.Bass("TRN2", target_bir_lowering=False)
    ap = lambda t: t.ap()

    def _lpr(v, out, in_):
        with nc.allow_low_precision(reason="bf16 softmax reciprocal broadcast"):
            return v.reciprocal(out, in_)

    xqT = ap(nc.dram_tensor("xqT", [D, NT * QW], F32, kind="ExternalInput"))
    xbT = ap(nc.dram_tensor("xbT", [D, S], F32, kind="ExternalInput"))
    wqT_d = ap(nc.dram_tensor("wqT", [D, D], F32, kind="ExternalInput"))
    wkT_d = ap(nc.dram_tensor("wkT", [D, D], F32, kind="ExternalInput"))
    wvT_d = ap(nc.dram_tensor("wvT", [D, D], F32, kind="ExternalInput"))
    woT_d = ap(nc.dram_tensor("woT", [D, D], F32, kind="ExternalInput"))
    fk_d = ap(nc.dram_tensor("fk", [D, D], BF16, kind="ExternalInput"))
    fbias = ap(nc.dram_tensor("fbias", [128, 8], BF16, kind="ExternalInput"))
    scl_d = ap(nc.dram_tensor("scl", [1, 2], F32, kind="ExternalInput"))
    maskh = ap(nc.dram_tensor("maskh", [128, 8 * QW], BF16, kind="ExternalInput"))
    cpool_d = ap(nc.dram_tensor("cpool", [128, len(CONSTS)], F32, kind="ExternalInput"))
    outT = ap(nc.dram_tensor("outT", [D, NT * QW], F32, kind="ExternalOutput"))
    kscr = ap(nc.dram_tensor("kscr", [8, 128, S], BF16, kind="Internal"))
    xb16 = ap(nc.dram_tensor("xb16", [6, 128, 4096], BF16, kind="Internal"))

    p = Prog(nc)
    with ExitStack() as ctx:
        e = ctx.enter_context
        cp_sb = e(nc.sbuf_tensor("cp_sb", [128, len(CONSTS)], F32))
        for i, val in enumerate(CONSTS):
            nc.const_aps.register(F32, val, cp_sb[:, i:i + 1])
        sclbc = e(nc.sbuf_tensor("sclbc", [128, 4], F32))
        bias_sb = e(nc.sbuf_tensor("bias_sb", [128, 8], BF16))
        biasf = e(nc.sbuf_tensor("biasf", [128, 8], F32))
        ones1 = e(nc.sbuf_tensor("ones1", [1, 128], F32))
        onesb = e(nc.sbuf_tensor("onesb", [128, 1], BF16))
        sclrow = e(nc.sbuf_tensor("sclrow", [1, 4], F32))
        mask_sb = e(nc.sbuf_tensor("mask_sb", [128, 8 * QW], BF16))
        arena = e(nc.sbuf_tensor("arena", [128, 16384], BF16))
        fkq = e(nc.sbuf_tensor("fkq", [128, 8 * D], FP8))
        zq = e(nc.sbuf_tensor("zq", [128, 8 * QW], FP8))
        v_sb = e(nc.sbuf_tensor("v_sb", [128, 32 * VP], BF16))
        bfp = e(nc.sbuf_tensor("bfp", [128, 25600], BF16))
        xqs = e(nc.sbuf_tensor("xqs", [128, 1024], F32))
        z_sb = e(nc.sbuf_tensor("z_sb", [128, 8 * QW], F32))
        fpp = e(nc.sbuf_tensor("fpp", [128, 512], F32))
        rfr = e(nc.sbuf_tensor("rfr", [33, 1024], BF16))
        rbc = e(nc.sbuf_tensor("rbc", [64, 2048], BF16))
        rowb = e(nc.sbuf_tensor("rowb", [1, 2048], F32))
        tmpo = e(nc.sbuf_tensor("tmpo", [64, 2048], BF16))

        XQB, XBF1, QT, ATT = 0, 4096, 4096, 8192
        PTE, PTM, KBF, ZB, SQ = 12288, 14336, 16384, 18432, 18944
        WOB, XST = 19456, 23552

        psS = e(nc.psum_tensor("psS", [128, 2048], F32))
        psA = psS[:, 0:1024]
        psB = psS[:, 1024:2048]
        av0 = e(nc.psum_tensor("av0", [128, 512], F32))
        av1 = e(nc.psum_tensor("av1", [128, 512], F32))
        psC = e(nc.psum_tensor("psC", [128, 1024], F32))
        pairs = [psA, psB]

        for s in ("te", "ve", "se", "dK0", "dK1", "dK2", "dK3", "dXQ",
                  "dKS0", "dKS1", "dO", "dM", "dT0", "dT1", "dF0", "dF1",
                  "dXB0", "dXB1", "dW", "dWq", "dWo", "dR", "dXQ1", "dXS0", "dXS1", "dWB",
                  "dR2", "dT2", "dT3", "dXC0", "dXC1"):
            p.add_sem(s, e(nc.semaphore(s)))

        T = lambda fn, deps=(): p.emit("tensor", fn, deps, inc="te")
        V = lambda fn, deps=(): p.emit("vector", fn, deps, inc="ve")
        SC = lambda fn, deps=(): p.emit("scalar", fn, deps, inc="se")

        def xpose_view(dram_ap, j):
            return dram_ap.rearrange("d (n t) -> d n t", t=QW)[:, j, :] \
                          .rearrange("(c p) t -> p c t", p=128)

        def sb3(ap2d):
            return ap2d.rearrange("p (c t) -> p c t", t=QW)

        def wview(dram_ap):
            return dram_ap.rearrange("(c p) o -> p c o", p=128)

        def aview(arena_slice):
            return arena_slice.rearrange("p (c o) -> p c o", o=1024)

        tok_dM = None
        for out_ap, in_ap in ((cp_sb[:], cpool_d), (sclrow[0:1, 0:2], scl_d),
                              (bias_sb[:], fbias), (mask_sb[:], maskh)):
            tok_dM = p.dma("sync", out_ap, in_ap, "dM")
        tok_wk = p.dma("gpsimd", aview(arena[:, 0:8192]), wview(wkT_d), "dW")
        tok_wv = p.dma("gpsimd", aview(arena[:, 8192:16384]), wview(wvT_d), "dW")
        WKV = ("dW", 32)

        t_biasf = V(lambda v: v.tensor_copy(biasf[:], bias_sb[:]), deps=[tok_dM])
        t_ones1 = V(lambda v: v.memset(ones1[:], 1.0))
        t_onesb = V(lambda v: v.memset(onesb[:], 1.0))
        t_vones = t_onesb
        for k in range(32):
            aph = v_sb[:, k * VP:(k + 1) * VP].rearrange("p (h d) -> p h d", h=H)[:, :, 64:65]
            t_vones = V(lambda v, aph=aph: v.memset(aph, 1.0))

        t_r0 = V(lambda v: v.reciprocal(sclrow[0:1, 2:3], sclrow[0:1, 0:1]), deps=[tok_dM])
        t_r1 = V(lambda v: v.reciprocal(sclrow[0:1, 3:4], sclrow[0:1, 1:2]), deps=[t_r0])
        t_r2 = V(lambda v: v.tensor_tensor(sclrow[0:1, 0:1], sclrow[0:1, 0:1],
                                           sclrow[0:1, 1:2], ALU.mult), deps=[t_r1])
        t_bs = T(lambda t: t.matmul(psC[:, 0:4], ones1[:], sclrow[0:1, 0:4],
                                    start=True, stop=True), deps=[t_ones1, t_r2])
        tq0 = V(lambda v: v.tensor_copy(sclbc[:, 0:1], psC[:, 2:3]), deps=[t_bs])
        tq1 = V(lambda v: v.tensor_copy(sclbc[:, 1:2], psC[:, 3:4]), deps=[tq0])
        t_sclbc = V(lambda v: v.tensor_copy(sclbc[:, 2:3], psC[:, 0:1]), deps=[tq1])
        psC_cons = [t_sclbc, t_sclbc]

        xbf_tok = [None, None]
        xbf_cons = [None, None]
        kev_cons = [None, None]
        psA_cons = [None, None]
        psB_cons = [None, None]
        dKS = ["dKS0", "dKS1"]
        dXB = ["dXB0", "dXB1"]
        XBF = [XQB, XBF1]
        t_kv_last = None
        t_k_last = None
        ph1_vready = [None, None]
        first_scalar_dep = tok_dM

        spill_tok = [None, None]
        xspill = [None] * 8
        for tt in range(8):
            s = tt % 2
            xbf_tok[s] = p.dma("gpsimd", sb3(bfp[:, XBF[s]:XBF[s] + 4096]),
                               xpose_view(xbT, tt), dXB[s],
                               deps=[xbf_cons[s], spill_tok[s]])
            if tt >= 2:
                spill_tok[s] = p.dma("sync", xb16[tt - 2],
                                     sb3(bfp[:, XBF[s]:XBF[s] + 4096]),
                                     "dXC0" if s == 0 else "dXC1",
                                     deps=[xbf_tok[s]])
                xspill[tt] = spill_tok[s]
            for oc in range(8):
                hA = oc % 2
                tok = None
                for ic in range(8):
                    st, sp = ic == 0, ic == 7
                    def mmk(t, oc=oc, ic=ic, s=s, hA=hA, st=st, sp=sp):
                        return t.matmul(psS[:, hA * 512:hA * 512 + 512],
                                        arena[:, ic * 1024 + oc * 128: ic * 1024 + oc * 128 + 128],
                                        bfp[:, XBF[s] + ic * 512: XBF[s] + ic * 512 + 512],
                                        start=st, stop=sp)
                    tok = T(mmk, deps=[xbf_tok[s], psA_cons[hA], WKV] if ic == 0 else ())
                t_kv_last = tok
                t_k_last = tok
                ks = oc % 2
                kp = SC(lambda sa, hA=hA, ks=ks:
                        sa.copy(bfp[:, PTE + ks * 1024: PTE + ks * 1024 + 512],
                                psS[:, hA * 512:hA * 512 + 512]),
                        deps=[tok, kev_cons[ks], first_scalar_dep])
                first_scalar_dep = None
                psA_cons[hA] = kp
                kev_cons[ks] = p.dma("sync", kscr[oc, :, tt * 512:(tt + 1) * 512],
                                     bfp[:, PTE + ks * 1024: PTE + ks * 1024 + 512],
                                     dKS[ks], deps=[kp])
                xbf_cons[s] = tok
            for ts4 in (range(4) if tt < 2 else []):
                for oh in range(2):
                    hB = (ts4 * 2 + oh) % 2
                    tok = None
                    for ic in range(8):
                        st, sp = ic == 0, ic == 7
                        def mmv(t, ic=ic, ts4=ts4, oh=oh, s=s, hB=hB, st=st, sp=sp):
                            return t.matmul(psS[:, 1024 + hB * 512:1024 + hB * 512 + 512],
                                            bfp[:, XBF[s] + ic * 512 + ts4 * 128:
                                                XBF[s] + ic * 512 + ts4 * 128 + 128],
                                            arena[:, 8192 + ic * 1024 + oh * 512:
                                                  8192 + ic * 1024 + oh * 512 + 512],
                                            start=st, stop=sp)
                        tok = T(mmv, deps=[xbf_tok[s], psB_cons[hB], WKV] if ic == 0 else ())
                    t_kv_last = tok
                    k = tt * 4 + ts4
                    def vev(v, k=k, oh=oh, hB=hB):
                        oap = v_sb[:, k * VP + oh * 8 * 65: k * VP + (oh * 8 + 8) * 65] \
                            .rearrange("p (h d) -> p h d", h=8)[:, :, 0:64]
                        iap = psS[:, 1024 + hB * 512:1024 + hB * 512 + 512].rearrange("p (h d) -> p h d", h=8)
                        return v.tensor_copy(oap, iap)
                    tok2 = V(vev, deps=[tok])
                    psB_cons[hB] = tok2
                    xbf_cons[s] = tok2
                    ph1_vready[tt] = tok2

        v_done = ("ve", p.count["ve"])

        tok_wq = p.dma("gpsimd", aview(arena[:, 0:8192]), wview(wqT_d), "dWq", deps=[t_k_last])
        WQC = tok_wq
        fkq_tok = None
        fst_cons = [t_kv_last, t_kv_last]
        dF = ["dF0", "dF1"]
        for ic in range(8):
            fs = ic % 2
            tokf = p.dma("sync", bfp[:, ATT + fs * 1024: ATT + fs * 1024 + 1024],
                         fk_d[ic * 128:(ic + 1) * 128, :], dF[fs], deps=[fst_cons[fs]])
            fkq_tok = SC(lambda sa, ic=ic, fs=fs:
                         sa.activation(fkq[:, ic * 1024:(ic + 1) * 1024],
                                       bfp[:, ATT + fs * 1024: ATT + fs * 1024 + 1024],
                                       AF.Copy, scale=sclbc[:, 1:2]),
                         deps=[tokf, t_sclbc, fkq_tok])
            fst_cons[fs] = fkq_tok

        from types import SimpleNamespace
        from collections import deque
        G = SimpleNamespace(
            xq_tok=None, xqb_tok=None, xqb_cons=[t_kv_last, spill_tok[0]],
            q_spill=spill_tok[1],
            qT_cons=None, qT_rdy=None, att_cons=fkq_tok,
            avp_cons=[[None, None], [None, None]],
            pair_cons=[list(filter(None, psA_cons)), list(filter(None, psB_cons))],
            pte_cons=[kev_cons[0], kev_cons[1]], ptm_cons=[None, None],
            kb_tok=[None, None, None, None], kb_cons=[None, None, None, None],
            vready=[ph1_vready[0], ph1_vready[1]] + [None] * 6, xqs_rd=[None, None], xqs_tok=[None] * 8,
            wob_cons=None, xst_tok=[None, None], xst_cons=[None, None],
            tmpo_cons=[None] * 4, rowb_cons=None, bfr_cons=None,
            rbc_cons=[None, None], srow_cons=None, lm_toks=[], pend_apply=[],
            vstage_tok={},
            zb_cons=fkq_tok, sq_cons=None, fpp_cons=[None, None],
            psC_cons=psC_cons, out_tok=None, res_toks=None, z2_toks=None,
            ln1_toks=None, q_toks=None, zq_gate=None, xq_gate=None,
        )
        dKb = ["dK0", "dK1", "dK2", "dK3"]
        dTm = ["dT0", "dT1", "dT2", "dT3"]
        dKS_full = [("dKS0", 16 * 32), ("dKS1", 16 * 32)]
        fill = deque()

        def emit_xqb(j):
            G.xqb_tok = p.dma("gpsimd", sb3(bfp[:, XQB:XQB + 4096]),
                              xpose_view(xqT, j), "dXB0", deps=[G.xqb_cons])

        def emit_q(j):
            qev = G.q_spill
            G.q_spill = None
            for oc in range(8):
                hC = oc % 2
                tok = None
                for ic in range(8):
                    st, sp = ic == 0, ic == 7
                    def mmq(t, oc=oc, ic=ic, hC=hC, st=st, sp=sp):
                        return t.matmul(psC[:, hC * 512:hC * 512 + 512],
                                        arena[:, ic * 1024 + oc * 128: ic * 1024 + oc * 128 + 128],
                                        bfp[:, XQB + ic * 512: XQB + ic * 512 + 512],
                                        start=st, stop=sp)
                    tok = T(mmq, deps=[G.xqb_tok, WQC, G.psC_cons[hC], G.qT_cons]
                            if ic == 0 else ())
                G.xqb_cons = tok
                qev = SC(lambda sa, oc=oc, hC=hC:
                         sa.copy(bfp[:, QT + oc * 512: QT + oc * 512 + 512],
                                 psC[:, hC * 512:hC * 512 + 512]),
                         deps=[tok, qev])
                G.psC_cons[hC] = qev
            G.qT_rdy = qev

        def emit_attention(j):
            n = NCH[j]
            G.lm_toks = []
            src = wview(woT_d)[:, :, 0:512]
            dst = bfp[:, WOB:WOB + 4096].rearrange("p (c o) -> p c o", o=512)
            G.wo_pre = p.dma("gpsimd", dst, src, "dWB",
                             deps=[G.wob_cons, t_kv_last])
            for gp in range(4):
                streams = [
                    (2 * gp, psA, PTE, PTM, (0, 1), 0),
                    (2 * gp + 1, psB, PTE + 1024, PTM + 1024, (2, 3), 1),
                ]
                pend = [None, None]
                avs = [[None, None], [None, None]]

                def flush_one(si):
                    if pend[si] is None:
                        return
                    c, g_, src_, st, sp, av_dep, par_ = pend[si]
                    pend[si] = None
                    if par_ == 0:
                        A0, A1 = av0[0:65, :], av1[0:65, :]
                    else:
                        A0, A1 = psC[0:65, 0:512], psC[0:65, 512:1024]
                    gate0 = ([G.avp_cons[par_][0]] +
                             ([G.psC_cons[0]] if par_ == 1 else [])) if st else []
                    gate1 = ([G.avp_cons[par_][1]] +
                             ([G.psC_cons[1]] if par_ == 1 else [])) if st else []
                    gate0 = gate0 + [G.vready[c // 4]]
                    gate1 = gate1 + ([G.vready[c // 4]] if st else [])
                    t_a0 = T(lambda t, c=c, g=g_, src=src_, st=st, sp=sp, A0=A0:
                             t.matmul(A0,
                                      v_sb[:, c * VP + (2 * g) * 65:
                                           c * VP + (2 * g) * 65 + 65],
                                      bfp[:, src:src + 512], start=st, stop=sp),
                             deps=[av_dep] + gate0)
                    t_a1 = T(lambda t, c=c, g=g_, src=src_, st=st, sp=sp, A1=A1:
                             t.matmul(A1,
                                      v_sb[:, c * VP + (2 * g + 1) * 65:
                                           c * VP + (2 * g + 1) * 65 + 65],
                                      bfp[:, src + 512:src + 1024], start=st, stop=sp),
                             deps=gate1)
                    if src_ >= PTM and src_ < KBF:
                        G.ptm_cons[si] = t_a1
                    else:
                        G.pte_cons[si] = t_a1
                    if sp:
                        avs[si] = [t_a0, t_a1]

                for b in range(n // 4):
                    kb_toks = []
                    for si, (g_, P, pte_o, ptm_o, kbs, par_) in enumerate(streams):
                        ks = kbs[b % 2]
                        G.kb_tok[ks] = p.dma(
                            "sync", bfp[:, KBF + ks * 512: KBF + ks * 512 + 512],
                            kscr[g_, :, b * 512:(b + 1) * 512], dKb[ks],
                            deps=[G.kb_cons[ks]] + dKS_full)
                    for c4 in range(4):
                        c = 4 * b + c4
                        so_toks = [None, None]
                        for si, (g_, P, pte_o, ptm_o, kbs, par_) in enumerate(streams):
                            ks = kbs[b % 2]
                            t_se = T(lambda t, ks=ks, c4=c4, g=g_, P=P:
                                     t.matmul(P[:, 0:512],
                                              bfp[0:64, KBF + ks * 512 + c4 * 128:
                                                  KBF + ks * 512 + c4 * 128 + 128],
                                              bfp[0:64, QT + g * 512: QT + g * 512 + 512],
                                              start=True, stop=True),
                                     deps=[G.kb_tok[ks], G.qT_rdy, G.pair_cons[si]])
                            t_so = T(lambda t, ks=ks, c4=c4, g=g_, P=P:
                                     t.matmul(P[:, 512:1024],
                                              bfp[64:128, KBF + ks * 512 + c4 * 128:
                                                  KBF + ks * 512 + c4 * 128 + 128],
                                              bfp[64:128, QT + g * 512: QT + g * 512 + 512],
                                              start=True, stop=True, tile_position=(64, 0)))
                            so_toks[si] = t_so
                            G.kb_cons[ks] = t_so
                        flush_one(0)
                        flush_one(1)
                        for si, (g_, P, pte_o, ptm_o, kbs, par_) in enumerate(streams):
                            t_exp = SC(lambda sa, P=P, pte_o=pte_o:
                                       sa.activation(bfp[:, pte_o:pte_o + 1024],
                                                     P[:, 0:1024], AF.Exp,
                                                     bias=-CSH, scale=0.125),
                                       deps=[so_toks[si], G.pte_cons[si]])
                            G.pair_cons[si] = t_exp
                            src_ = pte_o
                            if c >= n - 8:
                                m = c - (n - 8)
                                t_m0 = V(lambda v, src=src_, ptm_o=ptm_o, m=m:
                                         v.tensor_tensor(bfp[:, ptm_o:ptm_o + 512],
                                                         bfp[:, src:src + 512],
                                                         mask_sb[:, m * 512:(m + 1) * 512],
                                                         ALU.mult),
                                         deps=[t_exp, G.ptm_cons[si], tok_dM])
                                t_m1 = V(lambda v, src=src_, ptm_o=ptm_o, m=m:
                                         v.tensor_tensor(bfp[:, ptm_o + 512:ptm_o + 1024],
                                                         bfp[:, src + 512:src + 1024],
                                                         mask_sb[:, m * 512:(m + 1) * 512],
                                                         ALU.mult),
                                         deps=[t_m0])
                                G.pte_cons[si] = t_m1
                                av_dep = t_m1
                                src_ = ptm_o
                            else:
                                av_dep = t_exp
                            st, sp = c == 0, c == n - 1
                            pend[si] = (c, g_, src_, st, sp, av_dep, par_)
                        if c == 0:
                            for _ in range(n // 8 + 2):
                                if fill:
                                    fill.popleft()()
                flush_one(0)
                flush_one(1)
                ecs = [None, None]
                sss = [None, None]
                for si, (g_, P, pte_o, ptm_o, kbs, par_) in enumerate(streams):
                    if par_ == 0:
                        R0, R1 = av0[64:65, :], av1[64:65, :]
                        D0, D1 = av0[0:64, :], av1[0:64, :]
                    else:
                        R0, R1 = psC[64:65, 0:512], psC[64:65, 512:1024]
                        D0, D1 = psC[0:64, 0:512], psC[0:64, 512:1024]
                    so = si * 1024
                    t_s0 = V(lambda v, R0=R0, so=so:
                             v.tensor_copy(rowb[0:1, so:so + 512], R0),
                             deps=[avs[si][0], G.rowb_cons])
                    t_s1 = V(lambda v, R1=R1, so=so:
                             v.tensor_copy(rowb[0:1, so + 512:so + 1024], R1),
                             deps=[avs[si][1]])
                    sss[si] = (t_s0, t_s1)
                    t_ec0 = V(lambda v, g=g_, D0=D0: v.tensor_copy(
                        bfp[0:64, ATT + g * 512: ATT + g * 512 + 512], D0),
                        deps=[G.att_cons])
                    ts_ = g_ % 4
                    t_ec1 = V(lambda v, ts_=ts_, D1=D1: v.tensor_copy(
                        tmpo[0:64, ts_ * 512: ts_ * 512 + 512], D1),
                        deps=[G.tmpo_cons[ts_]])
                    ecs[si] = (t_ec0, t_ec1)
                    G.avp_cons[par_] = [t_ec0, t_ec1]
                    if par_ == 1:
                        G.psC_cons[0] = t_ec0
                        G.psC_cons[1] = t_ec1

                for ap_ in G.pend_apply:
                    ap_()
                G.pend_apply = []

                def mk_apply(si, g_, ts_, t_dR, ec0, ec1):
                    def go():
                        t_lo = V(lambda v: v.tensor_tensor(
                            bfp[0:64, ATT + g_ * 512: ATT + g_ * 512 + 512],
                            bfp[0:64, ATT + g_ * 512: ATT + g_ * 512 + 512],
                            rbc[0:64, si * 1024: si * 1024 + 512], ALU.mult),
                            deps=[t_dR, ec0])
                        t_up = V(lambda v: v.tensor_tensor(
                            tmpo[0:64, ts_ * 512: ts_ * 512 + 512],
                            tmpo[0:64, ts_ * 512: ts_ * 512 + 512],
                            rbc[0:64, si * 1024 + 512: si * 1024 + 1024],
                            ALU.mult), deps=[t_lo, ec1])
                        G.tmpo_cons[ts_] = p.dma(
                            "sync", bfp[64:128, ATT + g_ * 512: ATT + g_ * 512 + 512],
                            tmpo[0:64, ts_ * 512: ts_ * 512 + 512], dTm[ts_],
                            deps=[t_up, G.att_cons])
                        G.lm_toks.append(t_lo)
                        G.rbc_cons[si] = t_up
                    return go

                if gp < 3:
                    for si, (g_, P, pte_o, ptm_o, kbs, par_) in enumerate(streams):
                        row = 0 if si == 0 else 32
                        t_ra = V(lambda v, si=si, row=row: _lpr(
                            v, rfr[row:row + 1, 0:512],
                            rowb[0:1, si * 1024: si * 1024 + 512]),
                            deps=[sss[si][0]] + ([G.bfr_cons] if si == 0 else []))
                        t_rb = V(lambda v, si=si, row=row: _lpr(
                            v, rfr[row:row + 1, 512:1024],
                            rowb[0:1, si * 1024 + 512: si * 1024 + 1024]),
                            deps=[sss[si][1]])
                        t_dR = p.dma(
                            "gpsimd", rbc[0:64, si * 1024:(si + 1) * 1024],
                            bass.AP(rfr, row * 1024,
                                    [[1024, 1], [0, 64], [1, 1024]]),
                            "dR" if si == 0 else "dR2",
                            deps=[t_rb, G.rbc_cons[si]])
                        G.bfr_cons = t_dR
                        G.pend_apply.append(
                            mk_apply(si, g_, g_ % 4, t_dR, ecs[si][0], ecs[si][1]))
                else:
                    t_ln0 = SC(lambda sa: sa.activation(rowb[0:1, 0:1024],
                                                        rowb[0:1, 0:1024], AF.Ln),
                               deps=[sss[1][1], G.rowb_cons])
                    t_ex0 = SC(lambda sa: sa.activation(rowb[0:1, 0:1024],
                                                        rowb[0:1, 0:1024], AF.Exp,
                                                        scale=-1.0), deps=[t_ln0])
                    t_ln1 = SC(lambda sa: sa.activation(rowb[0:1, 1024:2048],
                                                        rowb[0:1, 1024:2048], AF.Ln),
                               deps=[t_ex0])
                    t_ex1 = SC(lambda sa: sa.activation(rowb[0:1, 1024:2048],
                                                        rowb[0:1, 1024:2048], AF.Exp,
                                                        scale=-1.0), deps=[t_ln1])
                    t_bA0 = T(lambda t: t.matmul(psC[0:64, 0:512], ones1[0:1, 0:64],
                                                 rowb[0:1, 0:512],
                                                 start=True, stop=True),
                              deps=[t_ex0, ecs[1][0]])
                    t_bB0 = T(lambda t: t.matmul(psC[0:64, 512:1024], ones1[0:1, 0:64],
                                                 rowb[0:1, 512:1024],
                                                 start=True, stop=True),
                              deps=[ecs[1][1]])
                    t_bA1 = T(lambda t: t.matmul(av0[0:64, 0:512], ones1[0:1, 0:64],
                                                 rowb[0:1, 1024:1536],
                                                 start=True, stop=True),
                              deps=[t_ex1, ecs[0][0]])
                    t_bB1 = T(lambda t: t.matmul(av1[0:64, 0:512], ones1[0:1, 0:64],
                                                 rowb[0:1, 1536:2048],
                                                 start=True, stop=True),
                              deps=[ecs[0][1]])
                    G.rowb_cons = t_bB1
                    for si, (g_, P, pte_o, ptm_o, kbs, par_) in enumerate(streams):
                        ts_ = g_ % 4
                        PL = psC[0:64, 0:512] if si == 0 else av0[0:64, 0:512]
                        PU = psC[0:64, 512:1024] if si == 0 else av1[0:64, 0:512]
                        bl = t_bA0 if si == 0 else t_bA1
                        bu = t_bB0 if si == 0 else t_bB1
                        t_lo = V(lambda v, g=g_, PL=PL: v.tensor_tensor(
                            bfp[0:64, ATT + g * 512: ATT + g * 512 + 512],
                            bfp[0:64, ATT + g * 512: ATT + g * 512 + 512],
                            PL, ALU.mult), deps=[bl, ecs[si][0]])
                        t_up = V(lambda v, ts_=ts_, PU=PU: v.tensor_tensor(
                            tmpo[0:64, ts_ * 512: ts_ * 512 + 512],
                            tmpo[0:64, ts_ * 512: ts_ * 512 + 512],
                            PU, ALU.mult), deps=[bu, ecs[si][1]])
                        G.tmpo_cons[ts_] = p.dma(
                            "sync", bfp[64:128, ATT + g_ * 512: ATT + g_ * 512 + 512],
                            tmpo[0:64, ts_ * 512: ts_ * 512 + 512], dTm[ts_],
                            deps=[t_up, G.att_cons])
                        G.lm_toks.append(t_lo)
                        if si == 0:
                            G.psC_cons[0] = t_lo
                            G.psC_cons[1] = t_up
                        else:
                            G.avp_cons[0] = [t_lo, t_up]
            G.qT_cons = ("te", p.count["te"])

        def emit_O(j):
            att_rdy = list(G.lm_toks) + [(dn, p.count[dn]) for dn in dTm]
            res_toks = []
            wo_tok = None
            for og in range(8):
                if og == 0:
                    wo_tok = G.wo_pre
                elif og == 4:
                    src = wview(woT_d)[:, :, 512:1024]
                    dst = bfp[:, WOB:WOB + 4096].rearrange("p (c o) -> p c o", o=512)
                    wo_tok = p.dma("gpsimd", dst, src, "dWB",
                                   deps=[G.wob_cons, t_kv_last])
                hC = og % 2
                AV = av0 if hC == 0 else av1
                tok = None
                for ic in range(8):
                    st, sp = ic == 0, ic == 7
                    def mmo(t, og=og, ic=ic, AV=AV, st=st, sp=sp):
                        return t.matmul(AV[:, 0:512],
                                        bfp[:, WOB + ic * 512 + (og % 4) * 128:
                                            WOB + ic * 512 + (og % 4) * 128 + 128],
                                        bfp[:, ATT + ic * 512: ATT + ic * 512 + 512],
                                        start=st, stop=sp)
                    tok = T(mmo, deps=(att_rdy + [wo_tok, G.avp_cons[0][hC]])
                            if ic == 0 else ())
                G.att_cons = tok
                if og % 4 == 3:
                    G.wob_cons = tok
                xp = og % 2
                xq_t = p.dma("sync", xqs[:, xp * 512:(xp + 1) * 512],
                             xqT[og * 128:(og + 1) * 128, j * QW:(j + 1) * QW],
                             "dXQ" if xp == 0 else "dXQ1",
                             deps=[G.xqs_rd[xp], G.out_tok])
                t_res = V(lambda v, og=og, AV=AV, xp=xp:
                          v.tensor_tensor(z_sb[:, og * 512:(og + 1) * 512],
                                          AV[:, 0:512],
                                          xqs[:, xp * 512:(xp + 1) * 512], ALU.add),
                          deps=[tok, xq_t, G.out_tok])
                G.xqs_rd[xp] = t_res
                G.avp_cons[0][hC] = t_res
                res_toks.append(t_res)
            G.res_toks = res_toks

        def step_ln_pieces(z_src_attr, quantize):
            gen_box = []

            def go_all():
                g = _ln_gen(z_src_attr, quantize)
                for _ in g:
                    pass
            return [go_all]

        def _ln_gen(z_src_attr, quantize):
            z_src = getattr(G, z_src_attr)
            s1tok = G.zb_cons
            for cidx in range(8):
                zc = SC(lambda sa, cidx=cidx:
                        sa.activation(bfp[:, ZB:ZB + 512],
                                      z_sb[:, cidx * 512:(cidx + 1) * 512], AF.Copy),
                        deps=[z_src[cidx], s1tok])
                s1tok = T(lambda t, cidx=cidx:
                          t.matmul(av0[0:1, 0:512], onesb[:], bfp[:, ZB:ZB + 512],
                                   start=(cidx == 0), stop=(cidx == 7)),
                          deps=[zc, t_onesb] + ([G.avp_cons[0][0]] if cidx == 0 else []))
            G.zb_cons = s1tok
            s2tok = G.sq_cons
            for cidx in range(8):
                sc_ = V(lambda v, cidx=cidx:
                        v.tensor_tensor(bfp[:, SQ:SQ + 512],
                                        z_sb[:, cidx * 512:(cidx + 1) * 512],
                                        z_sb[:, cidx * 512:(cidx + 1) * 512], ALU.mult),
                        deps=[z_src[cidx], s2tok])
                s2tok = T(lambda t, cidx=cidx:
                          t.matmul(av1[0:1, 0:512], onesb[:], bfp[:, SQ:SQ + 512],
                                   start=(cidx == 0), stop=(cidx == 7)),
                          deps=[sc_] + ([G.avp_cons[0][1]] if cidx == 0 else []))
            G.sq_cons = s2tok
            t_m = V(lambda v: v.tensor_scalar(rowb[0:1, 0:512], av0[0:1, 0:512],
                                              1.0 / 1024.0, None, ALU.mult),
                    deps=[s1tok, G.rowb_cons])
            t_vr = V(lambda v: v.tensor_scalar(rowb[0:1, 512:1024], av1[0:1, 0:512],
                                               1.0 / 1024.0, EPS, ALU.mult, ALU.add),
                     deps=[s2tok, t_m])
            G.avp_cons[0][0] = t_m
            G.avp_cons[0][1] = t_vr
            yield
            t_msq = V(lambda v: v.tensor_tensor(rowb[0:1, 1024:1536], rowb[0:1, 0:512],
                                                rowb[0:1, 0:512], ALU.mult),
                      deps=[t_vr])
            t_v2 = V(lambda v: v.tensor_tensor(rowb[0:1, 1536:2048], rowb[0:1, 512:1024],
                                               rowb[0:1, 1024:1536], ALU.subtract),
                     deps=[t_msq])
            t_ln = SC(lambda sa: sa.activation(rowb[0:1, 1024:1536], rowb[0:1, 1536:2048],
                                               AF.Ln), deps=[t_v2])
            t_rstd = SC(lambda sa: sa.activation(rowb[0:1, 512:1024], rowb[0:1, 1024:1536],
                                                 AF.Exp, scale=-0.5),
                        deps=[t_ln])
            t_mr = V(lambda v: v.tensor_tensor(rowb[0:1, 1536:2048], rowb[0:1, 0:512],
                                               rowb[0:1, 512:1024], ALU.mult),
                     deps=[t_rstd])
            t_bb0 = T(lambda t: t.matmul(av0[:, 0:512], ones1[:], rowb[0:1, 512:1024],
                                         start=True, stop=True),
                      deps=[t_rstd, t_m, G.avp_cons[0][0]])
            t_bb1 = T(lambda t: t.matmul(av1[:, 0:512], ones1[:], rowb[0:1, 1536:2048],
                                         start=True, stop=True),
                      deps=[t_mr, t_vr, G.avp_cons[0][1]])
            G.rowb_cons = t_bb1
            new_toks = []
            t_zq = None
            for cidx in range(8):
                fs = 0
                t_pa = V(lambda v, cidx=cidx, fs=fs:
                         v.tensor_tensor(fpp[:, fs * 512: fs * 512 + 512],
                                         z_sb[:, cidx * 512:(cidx + 1) * 512],
                                         av0[:, 0:512], ALU.mult),
                         deps=[t_bb0, G.fpp_cons[fs]] +
                              ([s1tok, s2tok] if cidx == 0 else []))
                t_pb = V(lambda v, cidx=cidx, fs=fs:
                         v.tensor_tensor(z_sb[:, cidx * 512:(cidx + 1) * 512],
                                         fpp[:, fs * 512: fs * 512 + 512],
                                         av1[:, 0:512], ALU.subtract),
                         deps=[t_pa, t_bb1])
                G.fpp_cons[fs] = t_pb
                new_toks.append(t_pb)
            G.avp_cons[0][0] = new_toks[-1]
            G.avp_cons[0][1] = new_toks[-1]
            if quantize:
                G.ln1_toks = new_toks
            else:
                G.out_chunks = new_toks
            yield
            if quantize:
                for cidx in range(8):
                    t_zb = V(lambda v, cidx=cidx:
                             v.tensor_copy(bfp[:, ZB:ZB + 512],
                                           z_sb[:, cidx * 512:(cidx + 1) * 512]),
                             deps=[new_toks[cidx], G.zb_cons])
                    t_zq = SC(lambda sa, cidx=cidx:
                              sa.activation(zq[:, cidx * 512:(cidx + 1) * 512],
                                            bfp[:, ZB:ZB + 512], AF.Copy,
                                            scale=sclbc[:, 0:1]),
                              deps=[t_zb, t_sclbc, G.zq_gate])
                    G.zb_cons = t_zq
                G.q_last = t_zq
            yield

        def step_vstage(T_, s4):
            def go():
                xs = s4 % 2
                src = xb16[T_ - 2].rearrange("p (c t) -> p c t", t=QW)[
                    :, :, s4 * 128:(s4 + 1) * 128]
                dst = bfp[:, XST + xs * 1024: XST + xs * 1024 + 1024] \
                    .rearrange("p (c t) -> p c t", t=128)
                G.vstage_tok[(T_, s4)] = p.dma(
                    "sync", dst, src, "dXS0" if xs == 0 else "dXS1",
                    deps=[G.xst_cons[xs], xspill[T_]])
            return go

        def step_vchain(T_, s4):
            def go():
                xs = s4 % 2
                tok_x = G.vstage_tok.pop((T_, s4))
                evs = []
                for oh in range(2):
                    AV = av0 if oh == 0 else av1
                    tok = None
                    for ic in range(8):
                        st, sp = ic == 0, ic == 7
                        def mmv2(t, ic=ic, oh=oh, xs=xs, AV=AV, st=st, sp=sp):
                            return t.matmul(AV[:, 0:512],
                                            bfp[:, XST + xs * 1024 + ic * 128:
                                                XST + xs * 1024 + ic * 128 + 128],
                                            arena[:, 8192 + ic * 1024 + oh * 512:
                                                  8192 + ic * 1024 + oh * 512 + 512],
                                            start=st, stop=sp)
                        tok = T(mmv2, deps=[tok_x, G.avp_cons[0][oh]] if ic == 0 else ())
                    k = T_ * 4 + s4

                    def vev2(v, k=k, oh=oh, AV=AV):
                        oap = v_sb[:, k * VP + oh * 8 * 65: k * VP + (oh * 8 + 8) * 65] \
                            .rearrange("p (h d) -> p h d", h=8)[:, :, 0:64]
                        iap = AV[:, 0:512].rearrange("p (h d) -> p h d", h=8)
                        return v.tensor_copy(oap, iap)
                    ev = V(vev2, deps=[tok])
                    G.avp_cons[0][oh] = ev
                    evs.append(ev)
                G.xst_cons[xs] = evs[-1]
                if s4 == 3:
                    G.vready[T_] = evs[-1]
            return go

        def append_weave(tiles):
            steps = [(T_, s4) for T_ in tiles for s4 in range(4)]
            si_, ci = 0, 0
            while ci < len(steps):
                if si_ < len(steps) and si_ - ci < 2:
                    fill.append(step_vstage(*steps[si_])); si_ += 1
                else:
                    fill.append(step_vchain(*steps[ci])); ci += 1

        def step_ffn(og):
            def go():
                hC = og % 2
                AV = av0 if hC == 0 else av1
                tok = None
                for ic in range(8):
                    st, sp = ic == 0, ic == 7
                    def mmf(t, og=og, ic=ic, AV=AV, st=st, sp=sp):
                        return t.matmul(AV[:, 0:512],
                                        fkq[:, ic * 1024 + og * 128:
                                            ic * 1024 + og * 128 + 128],
                                        zq[:, ic * 512:(ic + 1) * 512],
                                        start=st, stop=sp)
                    tok = T(mmf, deps=[G.q_last, fkq_tok, G.avp_cons[0][hC]] if ic == 0 else ())
                fs = 0
                t_yb = V(lambda v, og=og, AV=AV:
                         v.tensor_scalar(bfp[:, SQ:SQ + 512],
                                         AV[:, 0:512],
                                         sclbc[:, 2:3], biasf[:, og:og + 1],
                                         ALU.mult, ALU.add),
                         deps=[tok, G.sq_cons, t_biasf])
                t_z2 = V(lambda v, og=og, fs=fs:
                         v.tensor_tensor(fpp[:, fs * 512: fs * 512 + 512],
                                         z_sb[:, og * 512:(og + 1) * 512],
                                         bfp[:, SQ:SQ + 512], ALU.add),
                         deps=[t_yb, G.ln1_toks[og], G.fpp_cons[fs]])
                t_cp = V(lambda v, og=og, fs=fs:
                         v.tensor_copy(z_sb[:, og * 512:(og + 1) * 512],
                                       fpp[:, fs * 512: fs * 512 + 512]),
                         deps=[t_z2])
                G.fpp_cons[fs] = t_cp
                G.sq_cons = t_z2
                G.avp_cons[0][hC] = t_z2
                if og == 0:
                    G.z2_toks = []
                G.z2_toks.append(t_cp)
                if og == 7:
                    G.zq_gate = tok
            return go

        def step_out(j):
            def go():
                tok_o = p.dma("sync", xpose_view(outT, j), sb3(z_sb[:]), "dO",
                              deps=G.out_chunks)
                G.out_tok = tok_o
            return go

        emit_xqb(0)
        emit_q(0)
        emit_xqb(1)
        append_weave((2, 3))
        for j in range(NT):
            emit_attention(j)
            while fill:
                fill.popleft()()
            emit_O(j)
            if j + 1 < NT:
                emit_q(j + 1)
                if j + 2 < NT:
                    emit_xqb(j + 2)
            for piece in step_ln_pieces("res_toks", True):
                fill.append(piece)
            for og in range(8):
                fill.append(step_ffn(og))
            for piece in step_ln_pieces("z2_toks", False):
                fill.append(piece)
            fill.append(step_out(j))
            if j + 1 < NT and 2 * j + 4 < 8:
                append_weave((2 * j + 4, 2 * j + 5))
        while fill:
            fill.popleft()()

        p.emit("sync", lambda s: s.nop(), deps=[("dO", 16 * NT)])

        block = e(nc.Block())
        p.replay(block)
    return nc


_NC_CACHE = {}


def _get_nc():
    if "nc" not in _NC_CACHE:
        _NC_CACHE["nc"] = build_nc()
    return _NC_CACHE["nc"]


def _make_masks(half):
    r = np.arange(128)[:, None]
    c = np.arange(512)[None, :]
    tiles = []
    for m in range(8):
        if half == 0:
            t = (c - r >= 128 * m) if m < 4 else np.zeros((128, 512), bool)
        else:
            t = np.ones((128, 512), bool) if m < 4 else (c - r >= 128 * (m - 4))
        tiles.append(t.astype(ml_dtypes.bfloat16))
    return np.concatenate(tiles, axis=1)


def kernel(L0_attn_in_features, L0_attn_Wq, L0_attn_Wk, L0_attn_Wv, L0_attn_Wo,
           L0_ffn_kernel, L0_ffn_input_scale, L0_ffn_kernel_scale, L0_ffn_bias):
    x = np.asarray(L0_attn_in_features)
    wq = np.ascontiguousarray(np.asarray(L0_attn_Wq).T)
    wk = np.ascontiguousarray(np.asarray(L0_attn_Wk).T)
    wv = np.ascontiguousarray(np.asarray(L0_attn_Wv).T)
    wo = np.ascontiguousarray(np.asarray(L0_attn_Wo).T)
    fk = np.ascontiguousarray(np.asarray(L0_ffn_kernel))
    fbias = np.ascontiguousarray(np.asarray(L0_ffn_bias).reshape(8, 128).T)
    scl = np.concatenate([np.asarray(L0_ffn_input_scale),
                          np.asarray(L0_ffn_kernel_scale)]).reshape(1, 2).astype(np.float32)
    cpool = np.zeros((128, len(CONSTS)), np.float32)
    for i, val in enumerate(CONSTS):
        cpool[:, i] = val
    masks = [_make_masks(0), _make_masks(1)]

    in_maps = []
    for c in range(8):
        b, h = c // 2, c % 2
        xbT = np.ascontiguousarray(x[b].T)
        tiles = [2 * jj + h for jj in range(4)]
        xqT = np.ascontiguousarray(xbT.reshape(D, 8, QW)[:, tiles, :].reshape(D, NT * QW))
        in_maps.append({
            "xqT": xqT, "xbT": xbT, "wqT": wq, "wkT": wk, "wvT": wv, "woT": wo,
            "fk": fk, "fbias": fbias, "scl": scl, "maskh": masks[h], "cpool": cpool,
        })

    nc = _get_nc()
    res = run_bass_kernel_spmd(nc, in_maps, core_ids=list(range(8)))

    out = np.empty((4, S, D), np.float32)
    for c in range(8):
        b, h = c // 2, c % 2
        oT = np.asarray(res.results[c]["outT"])
        o = oT.T.reshape(NT, QW, D)
        for jj in range(NT):
            t = 2 * jj + h
            out[b, t * QW:(t + 1) * QW, :] = o[jj]
    return out
